# version 17 (final)
# speedup vs baseline: 1.1843x; 1.0665x over previous
import numpy as np
from contextlib import ExitStack

import concourse.bass as bass
import concourse.mybir as mybir
from concourse.bass_utils import run_bass_kernel_spmd

F32 = mybir.dt.float32
BF16 = mybir.dt.bfloat16
I32 = mybir.dt.int32
AF = mybir.ActivationFunctionType
ALU = mybir.AluOpType
AX = mybir.AxisListType

DBG_PHASE = 4
DBG_SCAN = 4
DBG_GROUPS = None

T = 4096
NT = 32
NG = 8
D = 1024
DC = 8
LN_EPS = 1e-5
PHI_EPS = 1e-6

ENGS = ("sync", "tensor", "vector", "scalar", "gpsimd")


class Sem:
    __slots__ = ("h", "val", "name")


class Env:
    def __init__(self, nc, ctx):
        self.nc = nc
        self.ctx = ctx
        self.q = {k: [] for k in ENGS}
        self._sems = {}

    def sem(self, name):
        s = self._sems.get(name)
        if s is None:
            s = Sem()
            s.h = self.ctx.enter_context(self.nc.semaphore(name))
            s.val = 0
            s.name = name
            self._sems[name] = s
        return s

    def sb(self, ctx, name, shape, dtype):
        return ctx.enter_context(self.nc.sbuf_tensor(name, shape, dtype))

    def ps(self, ctx, name, shape, dtype=F32):
        return ctx.enter_context(self.nc.psum_tensor(name, shape, dtype))

    def emit(self, eng, fn, wait=(), inc=None, n=1, tag=""):
        wl = tuple(wait)
        if inc is not None:
            inc.val += n
            tgt = (inc, n)
        else:
            tgt = None

        def thunk(e, fn=fn, wl=wl, tgt=tgt):
            for sm, v in wl:
                e.wait_ge(sm.h, v)
            inst = fn(e)
            if tgt is not None:
                inst.then_inc(tgt[0].h, tgt[1])

        thunk.meta = (eng, tuple((sm.name, v) for sm, v in wl),
                      (inc.name, n) if inc is not None else None, tag)
        self.q[eng].append(thunk)
        return inc.val if inc is not None else None

    def check_deadlock(self):
        pos = {e: 0 for e in ENGS}
        vals = {}
        import collections
        total = sum(len(q) for q in self.q.values())
        done = 0
        while done < total:
            progressed = False
            for e in ENGS:
                while pos[e] < len(self.q[e]):
                    eng, wl, inc, tag = self.q[e][pos[e]].meta
                    if any(vals.get(sn, 0) < v for sn, v in wl):
                        break
                    if inc is not None:
                        vals[inc[0]] = vals.get(inc[0], 0) + inc[1]
                    pos[e] += 1
                    done += 1
                    progressed = True
            if not progressed:
                print("DEADLOCK: engine positions:", pos)
                for e in ENGS:
                    if pos[e] < len(self.q[e]):
                        eng, wl, inc, tag = self.q[e][pos[e]].meta
                        blocked = [(sn, v, vals.get(sn, 0)) for sn, v in wl
                                   if vals.get(sn, 0) < v]
                        print(f"  {e}[{pos[e]}] tag={tag!r} blocked on "
                              f"{blocked} inc={inc}")
                return False
        return True

    def dma(self, eng, out, in_, wait=(), inc=None, **kw):
        return self.emit(eng, lambda e: e.dma_start(out=out, in_=in_, **kw),
                         wait, inc, 16)

    def barrier(self):
        bar = self.sem("bar")
        drains = tuple((s, s.val) for s in self._sems.values() if s is not bar)
        self.emit("sync", lambda e: e.nop(), wait=drains)
        target = bar.val + len(ENGS)
        for eng in ENGS:
            self.emit(eng, lambda e: e.nop(), inc=bar)
        for eng in ENGS:
            self.emit(eng, lambda e: e.nop(), wait=((bar, target),))

    def replay(self):
        block = self.ctx.enter_context(self.nc.Block())
        for eng in ENGS:
            thunks = self.q[eng]
            getattr(block, eng)(lambda e, thunks=thunks: [t(e) for t in thunks])


def tile(dram_ap, i, p=128):
    return dram_ap.rearrange("(n p) d -> n p d", p=p)[i]



def emit_consts(E, ctx):
    C = {}
    C["iota_r"] = E.sb(ctx, "iota_r", [128, 128], F32)
    C["iota_c"] = E.sb(ctx, "iota_c", [128, 1], F32)
    C["ident"] = E.sb(ctx, "ident_bf", [128, 128], BF16)
    C["tri"] = E.sb(ctx, "tri_bf", [128, 128], BF16)
    C["ones_f"] = E.sb(ctx, "ones_f", [1, 128], F32)
    C["ones_t"] = E.sb(ctx, "ones_t", [128, 128], F32)
    sg = E.sem("sgp")
    sd = E.sem("scst")
    v = E.emit("gpsimd", lambda e: e.iota(
        C["iota_r"][:], pattern=[[1, 128]], base=0, channel_multiplier=0,
        allow_small_or_imprecise_dtypes=True), inc=sg)
    v = E.emit("gpsimd", lambda e: e.iota(
        C["iota_c"][:], pattern=[[1, 1]], base=0, channel_multiplier=1,
        allow_small_or_imprecise_dtypes=True), inc=sg)
    E.emit("vector", lambda e: e.tensor_scalar(
        C["ident"][:], C["iota_r"][:], C["iota_c"][:], None, ALU.is_equal),
        wait=((sg, v),), inc=sd)
    E.emit("vector", lambda e: e.tensor_scalar(
        C["tri"][:], C["iota_r"][:], C["iota_c"][:], None, ALU.is_ge),
        wait=((sg, v),), inc=sd)
    E.emit("vector", lambda e: e.memset(C["ones_f"][:], 1.0), inc=sd)
    E.emit("vector", lambda e: e.memset(C["ones_t"][:], 1.0), inc=sd)
    C["ones_tb"] = E.sb(ctx, "ones_tb", [128, 128], BF16)
    E.emit("vector", lambda e: e.memset(C["ones_tb"][:], 1.0), inc=sd)
    C["tri512"] = E.sb(ctx, "tri512", [128, 512], BF16)
    E.emit("vector", lambda e: e.memset(C["tri512"][:, 128:512], 1.0), inc=sd)
    E.emit("vector", lambda e: e.tensor_scalar(
        C["tri512"][:, 0:128], C["iota_r"][:], C["iota_c"][:], None,
        ALU.is_ge), wait=((sg, v),), inc=sd)
    cvals = {"ln_eps": LN_EPS, "phi_eps": PHI_EPS, "l2_eps": 1e-12,
             "neg8": -8.0, "dq": 1.0 / 127.0, "one": 1.0}
    cv = E.sb(ctx, "cvals", [128, len(cvals)], F32)
    for i, (k, v) in enumerate(cvals.items()):
        E.emit("vector", lambda e, i=i, v=v: e.memset(cv[:, i:i + 1], v),
               inc=sd)
        C[k] = cv[:, i:i + 1]
    C["ready"] = (sd, sd.val)
    return C



def emit_ln_tile(E, C, st, col, x_ap, xn_out=None, wait_x=()):
    sa, sd = E.sem("lna"), E.sem("lnd")
    cs = slice(col, col + 1)
    prev = st.setdefault("_d1", {})
    wx = tuple(wait_x) + (((sd, prev[col]),) if prev.get(col) else ())
    a1 = E.emit("scalar", lambda e: e.activation(
        st["sq"][:], x_ap, AF.Square, accum_out=st["ssq"][:, cs]),
        wait=wx, inc=sa)
    v = E.emit("vector", lambda e: e.reduce_sum(
        st["sum"][:, cs], x_ap, axis=AX.X), wait=wait_x, inc=sd)
    v = E.emit("vector", lambda e: e.tensor_scalar_mul(
        st["mu"][:, cs], st["sum"][:, cs], 1.0 / D), wait=((sd, v),), inc=sd)
    v = E.emit("vector", lambda e: e.tensor_mul(
        st["mu2"][:, cs], st["mu"][:, cs], st["mu"][:, cs]),
        wait=((sd, v),), inc=sd)
    d1 = E.emit("vector", lambda e: e.scalar_tensor_tensor(
        st["var"][:, cs], st["ssq"][:, cs], 1.0 / D, st["mu2"][:, cs],
        ALU.mult, ALU.subtract), wait=((sa, a1), (sd, v)), inc=sd)
    prev[col] = d1
    aL = E.emit("scalar", lambda e: e.activation(
        st["lnv"][:, cs], st["var"][:, cs], AF.Ln, bias=C["ln_eps"]),
        wait=((sd, d1), C["ready"]), inc=sa)
    a2 = E.emit("scalar", lambda e: e.activation(
        st["rstd"][:, cs], st["lnv"][:, cs], AF.Exp, scale=-0.5),
        wait=((sa, aL),), inc=sa)
    d2 = None
    if xn_out is not None:
        d2 = E.emit("vector", lambda e: e.tensor_scalar(
            xn_out, x_ap, st["mu"][:, cs], st["rstd"][:, cs],
            ALU.subtract, ALU.mult), wait=((sa, a2), (sd, d1)), inc=sd)
    return a2, d2


def mk_ln_stats(E, ctx, name, ncols):
    st = {}
    for k in ("sum", "ssq", "mu", "mu2", "var", "lnv", "rstd"):
        st[k] = E.sb(ctx, f"{name}_{k}", [128, ncols], F32)
    st["sq"] = E.sb(ctx, f"{name}_sq", [128, D], F32)
    return st




def emit_l0_attn(E, C, x_src, x_dst, W):
    nc = E.nc
    with ExitStack() as c:
        wq = E.sb(c, "a0_wq", [128, DC * 1024], BF16)
        wk = E.sb(c, "a0_wk", [128, DC * 1024], BF16)
        wv = E.sb(c, "a0_wv", [128, DC * 1024], BF16)
        wo = E.sb(c, "a0_wo", [128, DC * 1024], BF16)
        xg = E.sb(c, "a0_xg", [128, 2 * 4 * 1024], F32)
        xng = E.sb(c, "a0_xng", [128, 4 * 1024], BF16)
        xnt = E.sb(c, "a0_xnt", [128, DC * 512], BF16)
        qft = E.sb(c, "a0_qft", [128, DC * 512], BF16)
        kft = E.sb(c, "a0_kft", [128, DC * 512], BF16)
        kfg = E.sb(c, "a0_kfg", [128, 4 * 1024], BF16)
        vag = E.sb(c, "a0_vag", [128, 4 * 16 * 65], BF16)
        oat = E.sb(c, "a0_oat", [128, DC * 512], BF16)
        tln = E.sb(c, "a0_tln", [128, 2 * 512], F32)
        rden_b = E.sb(c, "a0_rdenb", [128, 2 * 512], BF16)
        rbc = E.sb(c, "a0_rbc", [128, 2 * 512], F32)
        kvf = E.sb(c, "a0_kvf", [128, 8 * 65], F32)
        kvb = E.sb(c, "a0_kvb", [128, 8 * 65], BF16)
        tmpa = E.sb(c, "a0_tmpa", [128, 2 * 512], F32)
        tmpb = E.sb(c, "a0_tmpb", [128, 2 * 512], F32)
        s_sb = E.sb(c, "a0_ssb", [128, 3 * 512], BF16)
        outf = E.sb(c, "a0_outf", [128, 2 * 1024], F32)
        tmpr = E.sb(c, "a0_tmpr", [128, 2 * 1024], F32)
        st = mk_ln_stats(E, c, "a0_st", 8)

        ptr = [E.ps(c, "a0_ptr0", [128, 1024], BF16),
               E.ps(c, "a0_ptr1", [128, 1024], BF16)]
        ppj = [E.ps(c, "a0_ppj0", [128, 512], F32),
               E.ps(c, "a0_ppj1", [128, 512], F32),
               E.ps(c, "a0_ppj2", [128, 512], F32)]
        po2 = [E.ps(c, "a0_po2a", [128, 512], F32),
               E.ps(c, "a0_po2b", [128, 512], F32)]
        pbc = E.ps(c, "a0_pbc", [128, 512], F32)

        sw = E.sem("sw_l0")
        sx = [E.sem("sx0"), E.sem("sx1")]
        sst = E.sem("sst")
        smm = E.sem("smm")
        sev = E.sem("sev")
        sac = E.sem("sac")
        lna, lnd = E.sem("lna"), E.sem("lnd")

        for wsb, key in ((wq, "l0_wq"), (wk, "l0_wk"), (wv, "l0_wv"),
                         (wo, "l0_wo")):
            for ci in range(DC):
                E.dma("gpsimd", wsb[:, ci * 1024:(ci + 1) * 1024],
                      W[key][ci * 128:(ci + 1) * 128, :], inc=sw)
        w_ready = sw.val

        E.emit("vector", lambda e: e.memset(kvf[:], 0.0))
        kv0_ev = E.emit("vector", lambda e: e.memset(kvb[:], 0.0), inc=sev)
        vag4 = vag[:].rearrange("p (t h v) -> p t h v", t=4, h=16)
        E.emit("vector", lambda e: e.memset(vag4[:, :, :, 64:65], 1.0),
               inc=sev)
        kfg3 = kfg[:].rearrange("p (t d) -> p t d", t=4)

        S_ = {
            "tr_ev": [None, None],
            "ppj_free": [[], [], []],
            "s_used": [0, 0, 0],
            "po2_free": [(None, 0), (None, 0)],
            "pbc_free": (None, 0),
            "rden_free": [(None, 0), (None, 0)],
            "rbc_free": [(None, 0), (None, 0)],
            "oat_free": 0,
            "ln_prev": [0, 0],
            "xg_free": [0, 0],
            "phi_prev": [0, 0],
            "kvb_ev": [kv0_ev] * 8,
            "div_dv": 0,
            "xng_used": 0,
        }

        def xload(g):
            b = g % 2
            for tt in range(4):
                wl = ((sst, S_["xg_free"][b]),) if (tt == 0 and g > 1) else ()
                E.dma("sync", xg[:, (b * 4 + tt) * 1024:(b * 4 + tt + 1) * 1024],
                      tile(x_src, 4 * g + tt), wait=wl, inc=sx[b])

        def emit_ln_group(g, x_ready):
            b = g % 2
            ln_done = []
            for tt in range(4):
                xa = xg[:, (b * 4 + tt) * 1024:(b * 4 + tt + 1) * 1024]
                na = xng[:, tt * 1024:(tt + 1) * 1024]
                _, d2 = emit_ln_tile(
                    E, C, st, tt, xa, xn_out=na,
                    wait_x=((sx[b], x_ready), (smm, S_["xng_used"])))
                ln_done.append(d2)
            return ln_done

        def emit_xn_transpose(g, ln_done):
            for ci in range(DC):
                sl = ci % 2
                pt = ptr[sl]
                for tt in range(4):
                    src = xng[:, tt * 1024 + ci * 128: tt * 1024 + (ci + 1) * 128]
                    dst = pt[:, tt * 128:(tt + 1) * 128]
                    wl = [(lnd, ln_done[tt])]
                    if g == 0 and ci < 2 and tt == 0:
                        wl.append(C["ready"])
                    if S_["tr_ev"][sl]:
                        wl.append(S_["tr_ev"][sl])
                    E.emit("tensor", lambda e, d=dst, s=src: e.transpose(
                        d, s, C["ident"][:]), wait=wl,
                        inc=(smm if tt == 3 else None))
                mv = smm.val
                S_["tr_ev"][sl] = (sac, E.emit(
                    "scalar", lambda e, pt=pt, ci=ci: e.activation(
                        xnt[:, ci * 512:(ci + 1) * 512], pt[:, 0:512],
                        AF.Copy),
                    wait=((smm, mv),), inc=sac))
            S_["xng_used"] = smm.val
            return sac.val

        ngroups = NG
        xload(0)
        x_ready = sx[0].val
        ln_done0 = emit_ln_group(0, x_ready)
        xnt_ready = emit_xn_transpose(0, ln_done0)
        xload(1)
        nxt = {"x_ready": sx[1].val}

        for g in range(ngroups):
            b = g % 2
            qphi = [0] * DC
            kphi = [0] * DC
            for oc in range(DC):
                for wi, (wsb, dstt) in enumerate(((wq, qft), (wk, kft))):
                    sl = (2 * oc + wi) % 3
                    p = ppj[sl]
                    pb = (2 * oc + wi) % 2
                    for ci in range(DC):
                        lhs = wsb[:, ci * 1024 + oc * 128: ci * 1024 + (oc + 1) * 128]
                        rhs = xnt[:, ci * 512:(ci + 1) * 512]
                        wl = []
                        if ci == 0:
                            wl = [(sac, xnt_ready)] + list(S_["ppj_free"][sl])
                            if g == 0 and oc == 0:
                                wl.append((sw, w_ready))
                        E.emit("tensor", lambda e, p=p, l=lhs, r=rhs, ci=ci:
                               e.matmul(p[:], l, r, start=(ci == 0),
                                        stop=(ci == DC - 1)), wait=wl,
                               inc=(smm if ci == DC - 1 else None))
                    mv = smm.val
                    ta = tmpa[:, pb * 512:(pb + 1) * 512]
                    tb = tmpb[:, pb * 512:(pb + 1) * 512]
                    a1 = E.emit("scalar", lambda e, p=p, ta=ta: e.activation(
                        ta, p[:], AF.Abs),
                        wait=((smm, mv), (sev, S_["phi_prev"][pb])), inc=sac)
                    e1 = E.emit("vector", lambda e, p=p, tb=tb: e.tensor_scalar_max(
                        tb, p[:], 0.0), wait=((sac, a1),), inc=sev)
                    ax = E.emit("scalar", lambda e, ta=ta: e.activation(
                        ta, ta, AF.Exp, scale=-1.0), wait=((sac, a1),), inc=sac)
                    a2 = E.emit("scalar", lambda e, ta=ta: e.activation(
                        ta, ta, AF.Ln, bias=1.0), wait=((sac, ax),), inc=sac)
                    dst = dstt[:, oc * 512:(oc + 1) * 512]
                    e2 = E.emit("vector", lambda e, d=dst, ta=ta, tb=tb:
                                e.scalar_tensor_tensor(
                                    d, ta, PHI_EPS, tb, ALU.add, ALU.add),
                                wait=((sac, a2), (sev, e1)), inc=sev)
                    S_["phi_prev"][pb] = e2
                    S_["ppj_free"][sl] = [(sev, e1)]
                    if wi == 0:
                        qphi[oc] = e2
                    else:
                        kphi[oc] = e2

            for tt in range(4):
                for oc in range(2):
                    sl = (tt * 2 + oc) % 3
                    p = ppj[sl]
                    for ci in range(DC):
                        lhs = xnt[:, ci * 512 + tt * 128: ci * 512 + (tt + 1) * 128]
                        rhs = wv[:, ci * 1024 + oc * 512: ci * 1024 + (oc + 1) * 512]
                        wl = list(S_["ppj_free"][sl]) if ci == 0 else []
                        E.emit("tensor", lambda e, p=p, l=lhs, r=rhs, ci=ci:
                               e.matmul(p[:], l, r, start=(ci == 0),
                                        stop=(ci == DC - 1)), wait=wl,
                               inc=(smm if ci == DC - 1 else None))
                    mv = smm.val
                    dst = vag4[:, tt, oc * 8:(oc + 1) * 8, 0:64]
                    psrc = p[:].rearrange("p (h v) -> p h v", h=8)
                    ev = E.emit("vector", lambda e, d=dst, s=psrc: e.tensor_copy(
                        d, s), wait=((smm, mv),), inc=sev)
                    S_["ppj_free"][sl] = [(sev, ev)]
            v_ready = sev.val

            for ci in range(DC):
                sl = ci % 2
                pt = ptr[sl]
                for tt in range(4):
                    src = kft[:, ci * 512 + tt * 128: ci * 512 + (tt + 1) * 128]
                    dst = pt[:, tt * 128:(tt + 1) * 128]
                    wl = [(sev, kphi[ci])] if tt == 0 else []
                    if tt == 0 and S_["tr_ev"][sl]:
                        wl.append(S_["tr_ev"][sl])
                    E.emit("tensor", lambda e, d=dst, s=src: e.transpose(
                        d, s, C["ident"][:]), wait=wl,
                        inc=(smm if tt == 3 else None))
                mv = smm.val
                S_["tr_ev"][sl] = (sev, E.emit(
                    "vector", lambda e, pt=pt, ci=ci: e.tensor_copy(
                        kfg[:].rearrange("p (t d) -> p t d", t=4)
                        [:, :, ci * 128:(ci + 1) * 128],
                        pt[:, 0:512].rearrange("p (t d) -> p t d", d=128)),
                    wait=((smm, mv),), inc=sev))
            kf_ready = sev.val

            iters = [(h, kt) for h in range(16) for kt in range(4)]
            n_it = len(iters)
            gates = [None] * n_it
            backlog = []
            seq_ctr = [0]

            def sched(pos, fn):
                backlog.append((pos, seq_ctr[0], fn))
                seq_ctr[0] += 1

            def emit_S(j):
                h, kt = iters[j]
                sl = j % 3
                cc = h // 2
                rb = 64 * (h % 2)
                lhs = kft[rb:rb + 64,
                          cc * 512 + kt * 128: cc * 512 + (kt + 1) * 128]
                rhs = qft[rb:rb + 64, cc * 512 + kt * 128: (cc + 1) * 512]
                wl = list(S_["ppj_free"][sl])
                if kt == 0:
                    wl.append((sev, max(qphi[cc], kphi[cc])))
                mv = E.emit("tensor", lambda e, p=ppj[sl], l=lhs, r=rhs, kt=kt:
                            e.matmul(p[:, kt * 128:512], l, r,
                                     start=True, stop=True), wait=wl, inc=smm)
                wlm = [(smm, max(mv, S_["s_used"][sl]))]
                s3 = s_sb[:, sl * 512 + kt * 128: sl * 512 + 512]
                pd_ = ppj[sl][:, kt * 128:512]
                dg = E.emit("vector", lambda e, d=s3, s=pd_, kt=kt:
                            e.tensor_tensor(d, s, C["tri512"][:, 0:512 - kt * 128],
                                            ALU.mult), wait=wlm, inc=sev)
                gates[j] = ((sev, dg),)
                S_["ppj_free"][sl] = [(sev, dg)]

            def emit_O(i):
                h, kt = iters[i]
                sl = i % 3
                cc = h // 2
                rb = 64 * (h % 2)
                pO = po2[h % 2]
                if kt == 0:
                    wl = [(sev, S_["kvb_ev"][cc]), (sev, qphi[cc])]
                    if S_["po2_free"][h % 2][0] is not None:
                        wl.append(S_["po2_free"][h % 2])
                    E.emit("tensor", lambda e, p=pO,
                           l=kvb[rb:rb + 64, cc * 65:(cc + 1) * 65],
                           r=qft[rb:rb + 64, cc * 512:(cc + 1) * 512]:
                           e.matmul(p[0:65, :], l, r, start=True, stop=False),
                           wait=wl)
                wl = list(gates[i])
                mv = E.emit("tensor", lambda e, p=pO, l=vag4[:, kt, h, :],
                            r=s_sb[:, sl * 512 + kt * 128: sl * 512 + 512],
                            kt=kt, sp=(kt == 3):
                            e.matmul(p[0:65, kt * 128:512], l, r,
                                     start=False, stop=sp),
                            wait=wl, inc=smm)
                S_["s_used"][sl] = mv
                if kt == 3:
                    sched(i + 2, lambda h=h, o_mm=mv: div_head(h, o_mm))
                    if h % 2 == 1:
                        sched(i + 3, lambda hp=h // 2: kv_update(hp))

            def div_head(h, o_mm):
                cc = h // 2
                rb = 64 * (h % 2)
                hx = h % 2
                pO = po2[hx]
                rdb = rden_b[:, hx * 512:(hx + 1) * 512]
                tlb = tln[:, hx * 512:(hx + 1) * 512]
                rcb = rbc[:, hx * 512:(hx + 1) * 512]
                wl = [(smm, o_mm)]
                if S_["rden_free"][hx][0] is not None:
                    wl.append(S_["rden_free"][hx])
                v1 = E.emit("vector", lambda e, p=pO, d=rdb: e.tensor_copy(
                    d[64:65, :], p[64:65, :]), wait=wl, inc=sev)
                wl = [(sev, v1)]
                if S_["pbc_free"][0] is not None:
                    wl.append(S_["pbc_free"])
                v2 = E.emit("tensor", lambda e, s=rdb: e.matmul(
                    pbc[:], C["ones_tb"][64:65, :], s[64:65, :],
                    start=True, stop=True), wait=wl, inc=smm)
                S_["rden_free"][hx] = (smm, v2)
                wl = [(smm, v2)]
                if S_["rbc_free"][hx][0] is not None:
                    wl.append(S_["rbc_free"][hx])
                a4 = E.emit("scalar", lambda e, d=tlb: e.activation(
                    d, pbc[:], AF.Ln, bias=C["phi_eps"]),
                    wait=wl, inc=sac)
                S_["pbc_free"] = (sac, a4)
                a5 = E.emit("scalar", lambda e, s=tlb, d=rcb: e.activation(
                    d, s, AF.Exp, scale=-1.0),
                    wait=((sac, a4),), inc=sac)
                dst = oat[rb:rb + 64, cc * 512:(cc + 1) * 512]
                dv = E.emit("vector", lambda e, d=dst, p=pO, r=rcb:
                            e.tensor_tensor(d, p[0:64, :], r[0:64, :],
                                            ALU.mult),
                            wait=((sac, a5), (smm, S_["oat_free"])), inc=sev)
                S_["po2_free"][hx] = (sev, dv)
                S_["rbc_free"][hx] = (sev, dv)
                S_["div_dv"] = dv

            def kv_update(hp):
                cc = hp
                wl = [(sev, max(kf_ready, v_ready))]
                if S_["po2_free"][1][0] is not None:
                    wl.append(S_["po2_free"][1])
                first = True
                for sub in range(2):
                    h = 2 * hp + sub
                    for kt in range(4):
                        outp = po2[1][0:64, 0:65] if sub == 0 \
                            else po2[1][64:128, 0:65]
                        tp = None if sub == 0 else (0, 64)
                        E.emit("tensor", lambda e, o=outp,
                               l=kfg3[:, kt, h * 64:(h + 1) * 64],
                               r=vag4[:, kt, h, :],
                               st_=(kt == 0), sp=(kt == 3), tp=tp:
                               e.matmul(o, l, r, start=st_, stop=sp,
                                        tile_position=tp),
                               wait=(wl if first else ()),
                               inc=(smm if (sub == 1 and kt == 3) else None))
                        first = False
                kv_mm = smm.val
                v = E.emit("vector", lambda e, cc=cc: e.tensor_tensor(
                    kvf[:, cc * 65:(cc + 1) * 65],
                    kvf[:, cc * 65:(cc + 1) * 65],
                    po2[1][:, 0:65], ALU.add), wait=((smm, kv_mm),), inc=sev)
                pk = E.emit("vector", lambda e, cc=cc: e.tensor_copy(
                    kvb[:, cc * 65:(cc + 1) * 65],
                    kvf[:, cc * 65:(cc + 1) * 65]),
                    wait=((sev, v),), inc=sev)
                S_["kvb_ev"][hp] = pk
                S_["po2_free"][1] = (sev, v)

            emit_S(0)
            emit_S(1)
            for i in range(n_it):
                for ent in sorted([t for t in backlog if t[0] <= i],
                                  key=lambda t: (t[0], t[1])):
                    backlog.remove(ent)
                    ent[2]()
                if i + 2 < n_it:
                    emit_S(i + 2)
                emit_O(i)
                if i == 8 and g + 1 < ngroups:
                    nxt["ln_done"] = emit_ln_group(g + 1, nxt["x_ready"])
            for ent in sorted(backlog, key=lambda t: (t[0], t[1])):
                ent[2]()
            backlog.clear()

            if g + 1 < ngroups:
                xnt_ready = emit_xn_transpose(g + 1, nxt["ln_done"])

            div_last = S_["div_dv"]
            for tt in range(4):
                h2 = tt % 2
                of = outf[:, h2 * 1024:(h2 + 1) * 1024]
                ofr = tmpr[:, h2 * 1024:(h2 + 1) * 1024]
                ev_out = []
                for oc in range(2):
                    p = po2[oc]
                    for ci in range(DC):
                        lhs = oat[:, ci * 512 + tt * 128: ci * 512 + (tt + 1) * 128]
                        rhs = wo[:, ci * 1024 + oc * 512: ci * 1024 + (oc + 1) * 512]
                        wl = []
                        if ci == 0:
                            wl = [(sev, div_last)]
                            if S_["po2_free"][oc][0] is not None:
                                wl.append(S_["po2_free"][oc])
                        E.emit("tensor", lambda e, p=p, l=lhs, r=rhs, ci=ci:
                               e.matmul(p[:], l, r, start=(ci == 0),
                                        stop=(ci == DC - 1)), wait=wl,
                               inc=(smm if ci == DC - 1 else None))
                    mv = smm.val
                    wlv = [(smm, mv)]
                    if S_["ln_prev"][h2]:
                        wlv.append((lna, S_["ln_prev"][h2]))
                    ev = E.emit("vector", lambda e, d=of, oc=oc, p=p:
                                e.tensor_copy(d[:, oc * 512:(oc + 1) * 512],
                                              p[:]),
                                wait=wlv, inc=sev)
                    ev_out.append(ev)
                    S_["po2_free"][oc] = (sev, ev)
                a2, _ = emit_ln_tile(E, C, st, 4 + h2, of,
                                     wait_x=((sev, ev_out[-1]),))
                S_["ln_prev"][h2] = a2
                cs = slice(4 + h2, 5 + h2)
                xa = xg[:, (b * 4 + tt) * 1024:(b * 4 + tt + 1) * 1024]
                v = E.emit("vector", lambda e, d=ofr, s=of, cs=cs:
                           e.tensor_scalar(d, s, st["mu"][:, cs],
                                           st["rstd"][:, cs],
                                           ALU.subtract, ALU.mult),
                           wait=((lna, a2), (lnd, E.sem("lnd").val)), inc=sev)
                rv = E.emit("vector", lambda e, xa=xa, d=ofr: e.tensor_add(
                    xa, xa, d), wait=((sev, v),), inc=sev)
                E.dma("sync", tile(x_dst, 4 * g + tt), xa,
                      wait=((sev, rv),), inc=sst)
            S_["oat_free"] = smm.val
            S_["xg_free"][b] = sst.val
            if g + 2 < ngroups:
                xload(g + 2)
                nxt["x_ready"] = sx[g % 2].val




def emit_l0_moe(E, C, x_src, x_dst, W):
    NE = 10
    HC = 16
    with ExitStack() as c:
        w1 = E.sb(c, "m_w1", [128, DC * 2048], BF16)
        w3 = E.sb(c, "m_w3", [128, DC * 2048], BF16)
        w2 = E.sb(c, "m_w2", [128, HC * 1024], BF16)
        router = E.sb(c, "m_rt", [128, DC * 8], BF16)
        bias_bc = E.sb(c, "m_bias", [128, 8], F32)
        rb_sb = E.sb(c, "m_rb", [1, 8], F32)
        cw = E.sb(c, "m_cw", [128, NT * 8], F32)
        xg = E.sb(c, "m_xg", [128, 4 * 1024], F32)
        xng = E.sb(c, "m_xng", [128, 4 * 1024], BF16)
        xnt = E.sb(c, "m_xnt", [128, DC * 512], BF16)
        xbuf = [E.sb(c, "m_xb0", [128, DC * 512], BF16),
                E.sb(c, "m_xb1", [128, DC * 512], BF16)]
        hfm = E.sb(c, "m_hfm", [128, HC * 512], BF16)
        tmps = E.sb(c, "m_tmps", [128, 512], F32)
        contrib = E.sb(c, "m_ctb", [128, 8 * 512], F32)
        lg = E.sb(c, "m_lg", [128, 8], F32)
        lg2 = E.sb(c, "m_lg2", [128, 8], F32)
        mm1 = E.sb(c, "m_m1", [128, 4], F32)
        msk = E.sb(c, "m_msk", [128, 8], F32)
        msk2 = E.sb(c, "m_msk2", [128, 8], F32)
        acc_sb = E.sb(c, "m_acc", [128, 2 * 1024], F32)
        tmpr = E.sb(c, "m_tmpr", [128, 2 * 1024], F32)
        st = mk_ln_stats(E, c, "m_st", 8)

        srt = E.sem("sw_rt")
        sw13 = [E.sem("sw13a"), E.sem("sw13b")]
        sw2s = [E.sem("sw2a"), E.sem("sw2b")]
        sx = E.sem("sx")
        sst = E.sem("sst")
        smm = E.sem("smm")
        sev = E.sem("sev")
        sac = E.sem("sac")
        sxt = [E.sem("sxt0"), E.sem("sxt1")]
        ssp = E.sem("ssp")
        sad4 = [E.sem(f"sad{i}") for i in range(8)]
        lna, lnd = E.sem("lna"), E.sem("lnd")

        acc = W["moe_acc"]
        spill = W["xnt_spill"]
        spill_r = spill.rearrange("(c p) t -> p c t", p=128)

        def load_expert_w13(e, wait):
            sem = sw13[e % 2]
            for ci in range(DC):
                E.dma("gpsimd", w1[:, ci * 2048:(ci + 1) * 2048],
                      W["moe_w1t"][e, ci * 128:(ci + 1) * 128, :],
                      wait=wait, inc=sem)
                E.dma("gpsimd", w3[:, ci * 2048:(ci + 1) * 2048],
                      W["moe_w3t"][e, ci * 128:(ci + 1) * 128, :],
                      wait=wait, inc=sem)
            return (sem, sem.val)

        def load_expert_w2(e, wait):
            sem = sw2s[e % 2]
            for hc in range(HC):
                E.dma("gpsimd", w2[:, hc * 1024:(hc + 1) * 1024],
                      W["moe_w2t"][e, hc * 128:(hc + 1) * 128, :],
                      wait=wait, inc=sem)
            return (sem, sem.val)

        for ci in range(DC):
            E.dma("gpsimd", router[:, ci * 8:(ci + 1) * 8],
                  W["router_t"][ci * 128:(ci + 1) * 128, :], inc=srt)
        srb = E.sem("srb")
        E.dma("sync", rb_sb[:], W["router_bias"], inc=srb)
        rt_ready = srt.val
        rb_ready = srb.val
        w13_ready = load_expert_w13(0, ())
        w2_ready = load_expert_w2(0, ())

        with ExitStack() as cp:
            ptr = [E.ps(cp, "m_ptr0", [128, 1024], BF16),
                   E.ps(cp, "m_ptr1", [128, 1024], BF16)]
            pr = E.ps(cp, "m_pr", [128, 8], F32)
            pb = E.ps(cp, "m_pb", [128, 8], F32)

            bmm = E.emit("tensor", lambda e: e.matmul(
                pb[:], C["ones_f"][:], rb_sb[:], start=True, stop=True),
                wait=((srb, rb_ready), C["ready"]), inc=smm)
            bev = E.emit("vector", lambda e: e.tensor_copy(
                bias_bc[:], pb[:]), wait=((smm, bmm),), inc=sev)

            tr_ev = [0, 0]
            pr_free = (None, 0)
            ln_gate = 0
            for g in range(NG):
                for tt in range(4):
                    E.dma("sync", xg[:, tt * 1024:(tt + 1) * 1024],
                          tile(x_src, 4 * g + tt),
                          wait=(((lnd, ln_gate),) if g > 0 else ()), inc=sx)
                x_ready = sx.val
                ln_done = []
                for tt in range(4):
                    xa = xg[:, tt * 1024:(tt + 1) * 1024]
                    na = xng[:, tt * 1024:(tt + 1) * 1024]
                    _, d2 = emit_ln_tile(E, C, st, tt, xa, xn_out=na,
                                         wait_x=((sx, x_ready),))
                    ln_done.append(d2)
                for ci in range(DC):
                    pt = ptr[ci % 2]
                    for tt in range(4):
                        src = xng[:, tt * 1024 + ci * 128:
                                  tt * 1024 + (ci + 1) * 128]
                        dst = pt[:, tt * 128:(tt + 1) * 128]
                        wl = [(lnd, ln_done[tt])]
                        if tr_ev[ci % 2]:
                            wl.append((sev, tr_ev[ci % 2]))
                        E.emit("tensor", lambda e, d=dst, s=src: e.transpose(
                            d, s, C["ident"][:]), wait=wl,
                            inc=(smm if tt == 3 else None))
                    mv = smm.val
                    wle = [(smm, mv)]
                    if g > 0:
                        wle.append((ssp, spill_gate))
                    tr_ev[ci % 2] = E.emit(
                        "vector", lambda e, pt=pt, ci=ci: e.tensor_copy(
                            xnt[:, ci * 512:(ci + 1) * 512], pt[:, 0:512]),
                        wait=wle, inc=sev)
                xnt_ready = sev.val
                for ci in range(DC):
                    E.dma("sync", spill[ci * 128:(ci + 1) * 128,
                                        g * 512:(g + 1) * 512],
                          xnt[:, ci * 512:(ci + 1) * 512],
                          wait=((sev, xnt_ready),), inc=ssp)
                spill_gate = ssp.val
                for tt in range(4):
                    gt = 4 * g + tt
                    for ci in range(DC):
                        lhs = xnt[:, ci * 512 + tt * 128: ci * 512 + (tt + 1) * 128]
                        rhs = router[:, ci * 8:(ci + 1) * 8]
                        wl = []
                        if ci == 0:
                            wl = [(sev, xnt_ready), (srt, rt_ready)]
                            if pr_free[0] is not None:
                                wl.append((pr_free[0], pr_free[1]))
                        E.emit("tensor", lambda e, l=lhs, r=rhs, ci=ci:
                               e.matmul(pr[:], l, r, start=(ci == 0),
                                        stop=(ci == DC - 1)), wait=wl,
                               inc=(smm if ci == DC - 1 else None))
                    mv = smm.val
                    v = E.emit("vector", lambda e: e.tensor_tensor(
                        lg[:], pr[:], bias_bc[:], ALU.add),
                        wait=((smm, mv), (sev, bev)), inc=sev)
                    pr_free = (sev, v)
                    v = E.emit("vector", lambda e: e.reduce_max(
                        mm1[:, 0:1], lg[:], axis=AX.X), wait=((sev, v),),
                        inc=sev)
                    v = E.emit("vector", lambda e: e.tensor_scalar(
                        msk[:], lg[:], mm1[:, 0:1], None, ALU.is_ge),
                        wait=((sev, v),), inc=sev)
                    v = E.emit("vector", lambda e: e.scalar_tensor_tensor(
                        lg2[:], msk[:], -1e30, lg[:], ALU.mult, ALU.add),
                        wait=((sev, v),), inc=sev)
                    v = E.emit("vector", lambda e: e.reduce_max(
                        mm1[:, 1:2], lg2[:], axis=AX.X), wait=((sev, v),),
                        inc=sev)
                    v = E.emit("vector", lambda e: e.tensor_scalar(
                        msk2[:], lg2[:], mm1[:, 1:2], None, ALU.is_ge),
                        wait=((sev, v),), inc=sev)
                    v = E.emit("vector", lambda e: e.tensor_sub(
                        mm1[:, 2:3], mm1[:, 0:1], mm1[:, 1:2]),
                        wait=((sev, v),), inc=sev)
                    a = E.emit("scalar", lambda e: e.activation(
                        mm1[:, 3:4], mm1[:, 2:3], AF.Sigmoid),
                        wait=((sev, v),), inc=sac)
                    v = E.emit("vector", lambda e: e.tensor_scalar(
                        mm1[:, 2:3], mm1[:, 3:4], -1.0, 1.0, ALU.mult,
                        ALU.add), wait=((sac, a),), inc=sev)
                    v = E.emit("vector", lambda e: e.tensor_scalar(
                        msk[:], msk[:], mm1[:, 3:4], None, ALU.mult),
                        wait=((sev, v),), inc=sev)
                    v = E.emit("vector", lambda e, gt=gt: e.scalar_tensor_tensor(
                        cw[:, gt * 8:(gt + 1) * 8], msk2[:], mm1[:, 2:3],
                        msk[:], ALU.mult, ALU.add), wait=((sev, v),), inc=sev)
                ln_gate = E.sem("lnd").val
            pre_mm = smm.val
            pre_ev = sev.val
            spill_done = ssp.val

        pg = [None, None]
        pu = [None, None]
        po = [None, None]
        with ExitStack() as ce:
            pg[0] = E.ps(ce, "m_pg0", [128, 512], F32)
            pg[1] = E.ps(ce, "m_pg1", [128, 512], F32)
            pu[0] = E.ps(ce, "m_pu0", [128, 512], F32)
            pu[1] = E.ps(ce, "m_pu1", [128, 512], F32)
            po[0] = E.ps(ce, "m_po0", [128, 512], F32)
            po[1] = E.ps(ce, "m_po1", [128, 512], F32)

            xb_free = [0, 0]
            pg_free = [(None, 0), (None, 0)]
            pu_free = [(None, 0), (None, 0)]
            po_free = [(None, 0), (None, 0)]
            ctb_free = {}
            sxp = [E.sem("sxp0"), E.sem("sxp1")]
            apply_prev = [0, 0]
            ln_prev2 = [0, 0]
            ep_xg_free = 0
            acc_ready = None
            w13_mm = 0
            w2_mm = 0
            silu_prev = 0
            pr_done = (smm, pre_mm)

            for e in range(NE):
                E.dma("sync", xbuf[0][:].rearrange("p (c t) -> p c t", c=DC),
                      spill_r[:, :, 0:512],
                      wait=((ssp, spill_done), (smm, xb_free[0])),
                      inc=sxt[0])
                x0_ready = sxt[0].val
                xrdy = [x0_ready, 0]
                for s in range(NG):
                    sl = s % 2
                    if s + 1 < NG:
                        nsl = (s + 1) % 2
                        E.dma("sync",
                              xbuf[nsl][:].rearrange("p (c t) -> p c t", c=DC),
                              spill_r[:, :, (s + 1) * 512:(s + 2) * 512],
                              wait=((smm, xb_free[nsl]),), inc=sxt[nsl])
                        xrdy[nsl] = sxt[nsl].val
                    for h in range(HC):
                        hp = h % 2
                        for which, wsb, pp, pfree in (
                                ("g", w1, pg[hp], pg_free[hp]),
                                ("u", w3, pu[hp], pu_free[hp])):
                            for ci in range(DC):
                                lhs = wsb[:, ci * 2048 + h * 128:
                                          ci * 2048 + (h + 1) * 128]
                                rhs = xbuf[sl][:, ci * 512:(ci + 1) * 512]
                                wl = []
                                if ci == 0:
                                    wl = [(sxt[sl], xrdy[sl]),
                                          w13_ready]
                                    if e == 0 and s == 0 and h < 2:
                                        wl.append(pr_done)
                                        wl.append((sev, pre_ev))
                                    if pfree[0] is not None:
                                        wl.append(pfree)
                                E.emit("tensor", lambda e_, p=pp, l=lhs,
                                       r=rhs, ci=ci: e_.matmul(
                                           p[:], l, r, start=(ci == 0),
                                           stop=(ci == DC - 1)), wait=wl,
                                       inc=(smm if ci == DC - 1 else None))
                        mv = smm.val
                        a = E.emit("scalar", lambda e_, p=pg[hp]: e_.activation(
                            tmps[:], p[:], AF.Silu),
                            wait=((smm, mv), (sev, silu_prev)), inc=sac)
                        pg_free[hp] = (sac, a)
                        dv = E.emit("vector", lambda e_, p=pu[hp], h=h:
                                    e_.tensor_tensor(
                                        hfm[:, h * 512:(h + 1) * 512],
                                        p[:], tmps[:], ALU.mult),
                                    wait=((sac, a),), inc=sev)
                        pu_free[hp] = (sev, dv)
                        silu_prev = dv
                    w13_mm = smm.val
                    hfm_ready = sev.val
                    if s == NG - 1 and e + 1 < NE:
                        w13_ready = load_expert_w13(e + 1, ((smm, w13_mm),))
                    if e < NE - 1:
                        for u, (oc, tt) in enumerate(
                                (oc, tt) for oc in range(2) for tt in range(4)):
                            up = u % 2
                            for h in range(HC):
                                lhs = hfm[:, h * 512 + tt * 128:
                                          h * 512 + (tt + 1) * 128]
                                rhs = w2[:, h * 1024 + oc * 512:
                                         h * 1024 + (oc + 1) * 512]
                                wl = []
                                if h == 0:
                                    wl = [(sev, hfm_ready), w2_ready]
                                    if po_free[up][0] is not None:
                                        wl.append(po_free[up])
                                E.emit("tensor", lambda e_, p=po[up], l=lhs,
                                       r=rhs, h=h: e_.matmul(
                                           p[:], l, r, start=(h == 0),
                                           stop=(h == HC - 1)), wait=wl,
                                       inc=(smm if h == HC - 1 else None))
                            mv = smm.val
                            gt = s * 4 + tt
                            ct = contrib[:, (u % 8) * 512:(u % 8 + 1) * 512]
                            wl = [(smm, mv)]
                            if (u % 8) in ctb_free:
                                wl.append(ctb_free[u % 8])
                            if e < 8:
                                dv = E.emit("vector", lambda e_, d=ct, p=po[up],
                                            gt=gt, ex=e: e_.tensor_scalar(
                                                d, p[:],
                                                cw[:, gt * 8 + ex: gt * 8 + ex + 1],
                                                None, ALU.mult),
                                            wait=wl + [(sev, pre_ev)], inc=sev)
                            else:
                                dv = E.emit("vector", lambda e_, d=ct, p=po[up]:
                                            e_.tensor_copy(d, p[:]),
                                            wait=wl, inc=sev)
                            po_free[up] = (sev, dv)
                            dst = acc.rearrange("(n p) d -> n p d", p=128)[gt][
                                :, oc * 512:(oc + 1) * 512]
                            E.dma("gpsimd", dst, ct,
                                  wait=((sev, dv),), inc=sad4[u % 8],
                                  accum_op=(ALU.add if e > 0 else ALU.bypass))
                            ctb_free[u % 8] = (sad4[u % 8], sad4[u % 8].val)
                    else:
                        if s == 0:
                            acc_ready = [sm.val for sm in sad4]
                        for tt in range(4):
                            gt = s * 4 + tt
                            wl = []
                            if tt == 0 and s > 0:
                                wl.append((sst, ep_xg_free))
                            E.dma("sync", xg[:, tt * 1024:(tt + 1) * 1024],
                                  tile(x_src, gt), wait=wl, inc=sxp[1])
                        x_ready = sxp[1].val
                        for tt in range(4):
                            gt = s * 4 + tt
                            h2 = tt % 2
                            of = contrib[:, 2048 + h2 * 1024:
                                         2048 + (h2 + 1) * 1024]
                            asb = acc_sb[:, h2 * 1024:(h2 + 1) * 1024]
                            ofr = tmpr[:, h2 * 1024:(h2 + 1) * 1024]
                            wl = [(sad4[tt], acc_ready[tt]),
                                  (sad4[4 + tt], acc_ready[4 + tt])]
                            if apply_prev[h2]:
                                wl.append((sev, apply_prev[h2]))
                            E.dma("sync", asb, tile(acc, gt), wait=wl,
                                  inc=sxp[0])
                            asb_ready = sxp[0].val
                            ev2 = None
                            for oc in range(2):
                                up = oc
                                for h in range(HC):
                                    lhs = hfm[:, h * 512 + tt * 128:
                                              h * 512 + (tt + 1) * 128]
                                    rhs = w2[:, h * 1024 + oc * 512:
                                             h * 1024 + (oc + 1) * 512]
                                    wl = []
                                    if h == 0:
                                        wl = [(sev, hfm_ready), w2_ready]
                                        if po_free[up][0] is not None:
                                            wl.append(po_free[up])
                                    E.emit("tensor", lambda e_, p=po[up],
                                           l=lhs, r=rhs, h=h: e_.matmul(
                                               p[:], l, r, start=(h == 0),
                                               stop=(h == HC - 1)), wait=wl,
                                           inc=(smm if h == HC - 1 else None))
                                mv = smm.val
                                wl = [(smm, mv), (sxp[0], asb_ready)]
                                if ln_prev2[h2]:
                                    wl.append((lna, ln_prev2[h2]))
                                if apply_prev[h2]:
                                    wl.append((sev, apply_prev[h2]))
                                if (4 + 2 * h2 + oc) in ctb_free:
                                    wl.append(ctb_free[4 + 2 * h2 + oc])
                                ev2 = E.emit(
                                    "vector", lambda e_, d=of, p=po[up], a=asb,
                                    oc=oc: e_.tensor_tensor(
                                        d[:, oc * 512:(oc + 1) * 512], p[:],
                                        a[:, oc * 512:(oc + 1) * 512],
                                        ALU.add), wait=wl, inc=sev)
                                po_free[up] = (sev, ev2)
                            a2, _ = emit_ln_tile(E, C, st, 4 + h2, of,
                                                 wait_x=((sev, ev2),))
                            ln_prev2[h2] = a2
                            cs = slice(4 + h2, 5 + h2)
                            xa = xg[:, tt * 1024:(tt + 1) * 1024]
                            v = E.emit("vector", lambda e_, d=ofr, s=of,
                                       cs=cs: e_.tensor_scalar(
                                           d, s, st["mu"][:, cs],
                                           st["rstd"][:, cs],
                                           ALU.subtract, ALU.mult),
                                       wait=((lna, a2),
                                             (lnd, E.sem("lnd").val)),
                                       inc=sev)
                            apply_prev[h2] = v
                            rv = E.emit("vector", lambda e_, xa=xa, d=ofr:
                                        e_.tensor_add(xa, xa, d),
                                        wait=((sev, v), (sxp[1], x_ready)),
                                        inc=sev)
                            E.dma("sync", tile(x_dst, gt), xa,
                                  wait=((sev, rv),), inc=sst)
                        ep_xg_free = sst.val
                    w2_mm = smm.val
                    xb_free[sl] = smm.val
                if e + 1 < NE:
                    w2_ready = load_expert_w2(e + 1, ((smm, w2_mm),))




def emit_l1_attn(E, C, x_src, x_dst, W):
    NH = 8
    with ExitStack() as c:
        wqkv = E.sb(c, "t_wqkv", [128, DC * 1536], BF16)
        wout = E.sb(c, "t_wout", [128, 4 * 1024], BF16)
        lnw = E.sb(c, "t_lnw", [128, 8], F32)
        lnb = E.sb(c, "t_lnb", [128, 8], F32)
        qT = E.sb(c, "t_qT", [128, 4 * T], BF16)
        kT = E.sb(c, "t_kT", [128, 4 * T], BF16)
        vag = E.sb(c, "t_vag", [128, NT * NH * 65], BF16)
        xg = E.sb(c, "t_xg", [128, 4 * 1024], F32)
        xnt = E.sb(c, "t_xnt", [128, DC * 512], BF16)
        qs = E.sb(c, "t_qs", [128, 2 * 512], F32)
        sqs = E.sb(c, "t_sqs", [128, 2 * 512], F32)
        ssq8 = E.sb(c, "t_ssq8", [128, 2 * 8], F32)
        rn8 = E.sb(c, "t_rn8", [128, 2 * 8], F32)
        qn = E.sb(c, "t_qn", [128, 2 * 512], BF16)
        p_t = E.sb(c, "t_pt", [128, 3 * 512], BF16)
        oat = E.sb(c, "t_oat", [128, 4 * 512], BF16)
        tln = E.sb(c, "t_tln", [128, 2 * 512], F32)
        rden_b = E.sb(c, "t_rdenb", [128, 2 * 512], BF16)
        rbc = E.sb(c, "t_rbc", [128, 2 * 512], F32)
        st = mk_ln_stats(E, c, "t_st", 8)

        ppj = [E.ps(c, "t_ppj0", [128, 512], F32),
               E.ps(c, "t_ppj1", [128, 512], F32),
               E.ps(c, "t_ppj2", [128, 512], F32)]

        sw = E.sem("sw_l1a")
        sst = E.sem("sst")
        smm = E.sem("smm")
        sev = E.sem("sev")
        sac = E.sem("sac")
        sxp = [E.sem("sxp0"), E.sem("sxp1")]
        lna, lnd = E.sem("lna"), E.sem("lnd")

        for ci in range(DC):
            E.dma("gpsimd", wqkv[:, ci * 1536:(ci + 1) * 1536],
                  W["l1_wqkv"][ci * 128:(ci + 1) * 128, :], inc=sw)
        for ci in range(4):
            E.dma("gpsimd", wout[:, ci * 1024:(ci + 1) * 1024],
                  W["l1_wout"][ci * 128:(ci + 1) * 128, :], inc=sw)
        E.dma("gpsimd", lnw[:], W["l1_lnw"], inc=sw)
        E.dma("gpsimd", lnb[:], W["l1_lnb"], inc=sw)
        w_ready = sw.val

        vag4 = vag[:].rearrange("p (t h v) -> p t h v", t=NT, h=NH)
        ones_ev = E.emit("vector", lambda e: e.memset(
            vag4[:, :, :, 64:65], 1.0), inc=sev)

        with ExitStack() as cpre:
            ptr = [E.ps(cpre, "t_ptr0", [128, 1024], BF16),
                   E.ps(cpre, "t_ptr1", [128, 1024], BF16)]
            xng = E.sb(cpre, "t_xng", [128, 4 * 1024], BF16)
            tr_ev = [None, None]
            pj_free = [(None, 0), (None, 0), (None, 0)]
            ln_gate = 0
            l2_prev = [0, 0]
            qn_used = [0, 0]
            pend = []
            step = [0]

            def emit_qn_tr(part, g, tt, b2, e3):
                qnb = qn[:, b2 * 512:(b2 + 1) * 512]
                dstT = qT if part == 0 else kT
                pt = ptr[b2]
                for cc in range(4):
                    src = qnb[:, cc * 128:(cc + 1) * 128]
                    dst = pt[:, cc * 128:(cc + 1) * 128]
                    wl = [(sev, e3)]
                    if tr_ev[b2]:
                        wl.append(tr_ev[b2])
                    E.emit("tensor", lambda e, d=dst, s=src:
                           e.transpose(d, s, C["ident"][:]), wait=wl,
                           inc=(smm if cc == 3 else None))
                mv2 = smm.val
                qn_used[b2] = mv2
                col = g * 512 + tt * 128
                dst3 = dstT[:].rearrange(
                    "p (c t) -> p c t", c=4)[:, :, col:col + 128]
                src3 = pt[:, 0:512].rearrange("p (c t) -> p c t", c=4)
                tr_ev[b2] = (sev, E.emit(
                    "vector", lambda e, d=dst3, s=src3: e.tensor_copy(
                        d, s), wait=((smm, mv2),), inc=sev))

            def flush_pend(now):
                for ent in [p for p in pend if p[0] <= now]:
                    pend.remove(ent)
                    ent[1]()

            sxg = E.sem("sxg")
            for g in range(NG):
                for tt in range(4):
                    qn_ = "sync" if tt % 2 == 0 else "gpsimd"
                    wl = ((lnd, ln_gate),) if (g > 0 and tt < 2) else ()
                    E.dma(qn_, xg[:, tt * 1024:(tt + 1) * 1024],
                          tile(x_src, 4 * g + tt), wait=wl,
                          inc=(sxp[0] if tt % 2 == 0 else sxg))
                x_ready = sxp[0].val
                xg_ready = sxg.val
                ln_done = []
                for tt in range(4):
                    xa = xg[:, tt * 1024:(tt + 1) * 1024]
                    na = xng[:, tt * 1024:(tt + 1) * 1024]
                    _, d2 = emit_ln_tile(
                        E, C, st, tt, xa, xn_out=na,
                        wait_x=((sxp[0], x_ready), (sxg, xg_ready)))
                    ln_done.append(d2)
                ln_gate = E.sem("lnd").val
                for ci in range(DC):
                    flush_pend(step[0])
                    step[0] += 0
                    pt = ptr[ci % 2]
                    for tt in range(4):
                        src = xng[:, tt * 1024 + ci * 128: tt * 1024 + (ci + 1) * 128]
                        dst = pt[:, tt * 128:(tt + 1) * 128]
                        wl = [(lnd, ln_done[tt])]
                        if g == 0 and ci < 2 and tt == 0:
                            wl.append(C["ready"])
                            wl.append((sw, w_ready))
                        if tr_ev[ci % 2]:
                            wl.append(tr_ev[ci % 2])
                        E.emit("tensor", lambda e, d=dst, s=src: e.transpose(
                            d, s, C["ident"][:]), wait=wl,
                            inc=(smm if tt == 3 else None))
                    mv = smm.val
                    tr_ev[ci % 2] = (sev, E.emit(
                        "vector", lambda e, pt=pt, ci=ci: e.tensor_scalar(
                            xnt[:, ci * 512:(ci + 1) * 512], pt[:, 0:512],
                            lnw[:, ci:ci + 1], lnb[:, ci:ci + 1],
                            ALU.mult, ALU.add),
                        wait=((smm, mv),), inc=sev))
                xnt_ready = sev.val
                for part in range(3):
                    for tt in range(4):
                        flush_pend(step[0])
                        step[0] += 1
                        gt = 4 * g + tt
                        b = (part * 4 + tt) % 3
                        b2 = tt % 2
                        p = ppj[b]
                        for ci in range(DC):
                            lhs = xnt[:, ci * 512 + tt * 128: ci * 512 + (tt + 1) * 128]
                            rhs = wqkv[:, ci * 1536 + part * 512:
                                       ci * 1536 + (part + 1) * 512]
                            wl = []
                            if ci == 0:
                                wl = [(sev, xnt_ready)]
                                if pj_free[b][0] is not None:
                                    wl.append(pj_free[b])
                            E.emit("tensor", lambda e, p=p, l=lhs, r=rhs, ci=ci:
                                   e.matmul(p[:], l, r, start=(ci == 0),
                                            stop=(ci == DC - 1)), wait=wl,
                                   inc=(smm if ci == DC - 1 else None))
                        mv = smm.val
                        if part == 2:
                            dst = vag4[:, gt, :, 0:64]
                            psrc = p[:].rearrange("p (h v) -> p h v", h=8)
                            ev = E.emit("vector", lambda e, d=dst, s=psrc:
                                        e.tensor_copy(d, s),
                                        wait=((smm, mv), (sev, ones_ev)), inc=sev)
                            pj_free[b] = (sev, ev)
                        else:
                            qsb = qs[:, b2 * 512:(b2 + 1) * 512]
                            sqb = sqs[:, b2 * 512:(b2 + 1) * 512]
                            s8b = ssq8[:, b2 * 8:(b2 + 1) * 8]
                            r8b = rn8[:, b2 * 8:(b2 + 1) * 8]
                            qnb = qn[:, b2 * 512:(b2 + 1) * 512]
                            a1 = E.emit("scalar", lambda e, p=p, d=sqb: e.activation(
                                d, p[:], AF.Square),
                                wait=((smm, mv), (sev, l2_prev[b2])), inc=sac)
                            e1 = E.emit("vector", lambda e, p=p, d=qsb: e.tensor_copy(
                                d, p[:]), wait=((sac, a1),), inc=sev)
                            pj_free[b] = (sev, e1)
                            e2 = E.emit("vector", lambda e, s=sqb, d=s8b: e.reduce_sum(
                                d.rearrange("p (a b) -> p a b", b=1),
                                s.rearrange("p (a b) -> p a b", b=64),
                                axis=AX.X), wait=((sac, a1), (sev, e1)), inc=sev)
                            a2 = E.emit("scalar", lambda e, s=s8b, d=r8b: e.activation(
                                d, s, AF.Ln, bias=C["l2_eps"]),
                                wait=((sev, e2),), inc=sac)
                            a3 = E.emit("scalar", lambda e, d=r8b: e.activation(
                                d, d, AF.Exp, scale=-0.5),
                                wait=((sac, a2),), inc=sac)
                            e3 = E.emit("vector", lambda e, q=qsb, r=r8b, d=qnb:
                                        e.tensor_tensor(
                                            d.rearrange("p (a b) -> p a b", b=64),
                                            q.rearrange("p (a b) -> p a b", b=64),
                                            r.rearrange("p (a b) -> p a b", b=1)
                                            .broadcast_to((128, 8, 64)), ALU.mult),
                                wait=((sac, a3), (smm, qn_used[b2])), inc=sev)
                            l2_prev[b2] = e3
                            pend.append((step[0] + 1,
                                         lambda part=part, g=g, tt=tt, b2=b2,
                                         e3=e3: emit_qn_tr(part, g, tt, b2, e3)))
                flush_pend(step[0] + 10)
            kv_ready = sev.val
            pre_lnd = E.sem("lnd").val
            pre_lna = E.sem("lna").val

        qT4 = qT[:].rearrange("p (c t) -> p c t", c=4)
        kT4 = kT[:].rearrange("p (c t) -> p c t", c=4)
        with ExitStack() as cf:
            po2 = [E.ps(cf, "t_po2a", [128, 512], F32),
                   E.ps(cf, "t_po2b", [128, 512], F32)]
            pbc = E.ps(cf, "t_pbc", [128, 512], F32)
            pout = [E.ps(cf, "t_pouta", [128, 512], F32),
                    E.ps(cf, "t_poutb", [128, 512], F32)]
            outf = E.sb(cf, "t_outf", [128, 2 * 1024], F32)
            tmpr = E.sb(cf, "t_tmpr", [128, 2 * 1024], F32)

            iters = []
            for qg in range(NG):
                for h in range(NH):
                    for kt in range(4 * qg + 4):
                        iters.append((qg, h, kt))
            n_it = len(iters)
            gates = [None] * n_it
            S_ = {
                "ppj_free": [pj_free[0], pj_free[1], pj_free[2]],
                "p_used": [0, 0, 0],
                "po2_free": [(None, 0), (None, 0)],
                "pout_free": [(None, 0), (None, 0)],
                "pbc_free": (None, 0),
                "rden_free": [(None, 0), (None, 0)],
                "rbc_free": [(None, 0), (None, 0)],
                "oat_free": 0,
                "ln_prev": [0, 0],
                "xg_free": 0,
                "div_dv": {},
            }
            backlog = []
            seq_ctr = [0]

            def sched(pos, fn):
                backlog.append((pos, seq_ctr[0], fn))
                seq_ctr[0] += 1

            def emit_S(j):
                qg, h, kt = iters[j]
                sl = j % 3
                cc = h // 2
                rb = 64 * (h % 2)
                lhs = kT4[:, cc, kt * 128:(kt + 1) * 128][rb:rb + 64, :]
                rhs = qT4[:, cc, qg * 512:(qg + 1) * 512][rb:rb + 64, :]
                wl = []
                if j < 3:
                    wl.append((sev, kv_ready))
                if S_["ppj_free"][sl][0] is not None:
                    wl.append(S_["ppj_free"][sl])
                mv = E.emit("tensor", lambda e, p=ppj[sl], l=lhs, r=rhs:
                            e.matmul(p[:], l, r, start=True, stop=True),
                            wait=wl, inc=smm)
                wl = [(smm, max(mv, S_["p_used"][sl]))]
                a = E.emit("scalar", lambda e, p=ppj[sl], sl=sl:
                           e.activation(
                               p_t[:, sl * 512:(sl + 1) * 512], p[:],
                               AF.Exp, bias=C["neg8"], scale=8.0),
                           wait=wl, inc=sac)
                S_["ppj_free"][sl] = (sac, a)
                jd = kt - 4 * qg
                if jd >= 0:
                    if jd > 0:
                        E.emit("vector", lambda e, sl=sl, jd=jd:
                               e.memset(p_t[:, sl * 512:
                                            sl * 512 + jd * 128], 0.0),
                               wait=((sac, a),))
                    dd = p_t[:, sl * 512 + jd * 128: sl * 512 + (jd + 1) * 128]
                    mk = E.emit("vector", lambda e, d=dd:
                                e.tensor_tensor(d, d, C["tri"][:], ALU.mult),
                                wait=((sac, a),), inc=sev)
                    gates[j] = (sev, mk)
                else:
                    gates[j] = (sac, a)

            def emit_O(i):
                qg, h, kt = iters[i]
                sl = i % 3
                pO = po2[h % 2]
                nkt = 4 * qg + 4
                wl = [gates[i]]
                if kt == 0 and S_["po2_free"][h % 2][0] is not None:
                    wl.append(S_["po2_free"][h % 2])
                mv = E.emit("tensor", lambda e, p=pO, l=vag4[:, kt, h, :],
                            r=p_t[:, sl * 512:(sl + 1) * 512],
                            st_=(kt == 0), sp=(kt == nkt - 1):
                            e.matmul(p[0:65, :], l, r, start=st_, stop=sp),
                            wait=wl, inc=smm)
                S_["p_used"][sl] = mv
                if kt == nkt - 1:
                    sched(i + 2, lambda qg=qg, h=h, o_mm=mv: div_head(qg, h, o_mm))
                    if h == NH - 1:
                        sched(i + 4, lambda qg=qg: outproj(qg))
                        if qg + 1 < NG:
                            sched(i + 5, lambda qg=qg: xload(qg + 1))

            def div_head(qg, h, o_mm):
                cc = h // 2
                rb = 64 * (h % 2)
                hx = h % 2
                pO = po2[hx]
                rdb = rden_b[:, hx * 512:(hx + 1) * 512]
                tlb = tln[:, hx * 512:(hx + 1) * 512]
                rcb = rbc[:, hx * 512:(hx + 1) * 512]
                wl = [(smm, o_mm)]
                if S_["rden_free"][hx][0] is not None:
                    wl.append(S_["rden_free"][hx])
                v1 = E.emit("vector", lambda e, p=pO, d=rdb: e.tensor_copy(
                    d[64:65, :], p[64:65, :]), wait=wl, inc=sev)
                wl = [(sev, v1)]
                if S_["pbc_free"][0] is not None:
                    wl.append(S_["pbc_free"])
                v2 = E.emit("tensor", lambda e, s=rdb: e.matmul(
                    pbc[:], C["ones_tb"][64:65, :], s[64:65, :],
                    start=True, stop=True), wait=wl, inc=smm)
                S_["rden_free"][hx] = (smm, v2)
                wl = [(smm, v2)]
                if S_["rbc_free"][hx][0] is not None:
                    wl.append(S_["rbc_free"][hx])
                a4 = E.emit("scalar", lambda e, d=tlb: e.activation(
                    d, pbc[:], AF.Ln), wait=wl, inc=sac)
                S_["pbc_free"] = (sac, a4)
                a5 = E.emit("scalar", lambda e, s=tlb, d=rcb: e.activation(
                    d, s, AF.Exp, scale=-1.0),
                    wait=((sac, a4),), inc=sac)
                dst = oat[rb:rb + 64, cc * 512:(cc + 1) * 512]
                dv = E.emit("vector", lambda e, d=dst, p=pO, r=rcb:
                            e.tensor_tensor(d, p[0:64, :],
                                            r[0:64, :], ALU.mult),
                            wait=((sac, a5), (smm, S_["oat_free"])), inc=sev)
                S_["po2_free"][hx] = (sev, dv)
                S_["rbc_free"][hx] = (sev, dv)
                S_["div_dv"][qg] = dv

            def xload(qg):
                for tt in range(4):
                    wl = ((sst, S_["xg_free"]),) if tt == 0 else ()
                    E.dma("sync", xg[:, tt * 1024:(tt + 1) * 1024],
                          tile(x_src, 4 * qg + tt), wait=wl, inc=sxp[1])

            def outproj(qg):
                oat_rd = S_["div_dv"][qg]
                x_ready = sxp[1].val
                for tt in range(4):
                    h2 = tt % 2
                    of = outf[:, h2 * 1024:(h2 + 1) * 1024]
                    ofr = tmpr[:, h2 * 1024:(h2 + 1) * 1024]
                    ev_out = []
                    for oc in range(2):
                        p = pout[oc]
                        for cc2 in range(4):
                            lhs = oat[:, cc2 * 512 + tt * 128:
                                      cc2 * 512 + (tt + 1) * 128]
                            rhs = wout[:, cc2 * 1024 + oc * 512:
                                       cc2 * 1024 + (oc + 1) * 512]
                            wl = []
                            if cc2 == 0:
                                wl = [(sev, oat_rd)]
                                if S_["pout_free"][oc][0] is not None:
                                    wl.append(S_["pout_free"][oc])
                            E.emit("tensor", lambda e, p=p, l=lhs, r=rhs,
                                   ci=cc2: e.matmul(p[:], l, r,
                                                    start=(ci == 0),
                                                    stop=(ci == 3)),
                                   wait=wl,
                                   inc=(smm if cc2 == 3 else None))
                        mv = smm.val
                        wlv = [(smm, mv)]
                        if S_["ln_prev"][h2]:
                            wlv.append((lna, S_["ln_prev"][h2]))
                        ev = E.emit("vector", lambda e, d=of, oc=oc, p=p:
                                    e.tensor_copy(
                                        d[:, oc * 512:(oc + 1) * 512], p[:]),
                                    wait=wlv, inc=sev)
                        ev_out.append(ev)
                        S_["pout_free"][oc] = (sev, ev)
                    a2, _ = emit_ln_tile(E, C, st, 4 + h2, of,
                                         wait_x=((sev, ev_out[-1]),))
                    S_["ln_prev"][h2] = a2
                    cs = slice(4 + h2, 5 + h2)
                    xa = xg[:, tt * 1024:(tt + 1) * 1024]
                    v = E.emit("vector", lambda e, d=ofr, s=of, cs=cs:
                               e.tensor_scalar(d, s, st["mu"][:, cs],
                                               st["rstd"][:, cs],
                                               ALU.subtract, ALU.mult),
                               wait=((lna, a2), (lnd, E.sem("lnd").val)),
                               inc=sev)
                    rv = E.emit("vector", lambda e, xa=xa, d=ofr:
                                e.tensor_add(xa, xa, d),
                                wait=((sev, v), (sxp[1], x_ready)), inc=sev)
                    E.dma("sync", tile(x_dst, 4 * qg + tt), xa,
                          wait=((sev, rv),), inc=sst)
                S_["oat_free"] = smm.val
                S_["xg_free"] = sst.val

            for tt in range(4):
                wl = ((lnd, pre_lnd), (lna, pre_lna)) if tt == 0 else ()
                E.dma("sync", xg[:, tt * 1024:(tt + 1) * 1024],
                      tile(x_src, tt), wait=wl, inc=sxp[1])

            emit_S(0)
            emit_S(1)
            for i in range(n_it):
                for ent in sorted([b for b in backlog if b[0] <= i],
                                  key=lambda t: (t[0], t[1])):
                    backlog.remove(ent)
                    ent[2]()
                if i + 2 < n_it:
                    emit_S(i + 2)
                emit_O(i)
            for ent in sorted(backlog, key=lambda t: (t[0], t[1])):
                ent[2]()
            backlog.clear()




def emit_l1_ffn(E, C, x_src, x_dst, W):
    HC3 = 24
    NG2 = 4
    with ExitStack() as c:
        wd = E.sb(c, "f_wd", [128, HC3 * 1024], BF16)
        wg_b = [E.sb(c, "f_wg0", [128, DC * 128], BF16),
                E.sb(c, "f_wg1", [128, DC * 128], BF16)]
        wu_b = [E.sb(c, "f_wu0", [128, DC * 128], BF16),
                E.sb(c, "f_wu1", [128, DC * 128], BF16)]
        xnt = E.sb(c, "f_xnt", [128, DC * 1024], BF16)
        hfm = E.sb(c, "f_hfm", [128, HC3 * 1024], BF16)
        xg = E.sb(c, "f_xg", [128, 4 * 1024], F32)
        xng = E.sb(c, "f_xng", [128, 4 * 1024], BF16)
        tmps = E.sb(c, "f_tmps", [128, 512], F32)
        outf = E.sb(c, "f_outf", [128, 2 * 1024], F32)
        tmpr = E.sb(c, "f_tmpr", [128, 2 * 1024], F32)
        st = mk_ln_stats(E, c, "f_st", 8)

        ptr = [E.ps(c, "f_ptr0", [128, 1024], BF16),
               E.ps(c, "f_ptr1", [128, 1024], BF16)]
        pg = [E.ps(c, "f_pg0", [128, 512], F32),
              E.ps(c, "f_pg1", [128, 512], F32)]
        pu = [E.ps(c, "f_pu0", [128, 512], F32),
              E.ps(c, "f_pu1", [128, 512], F32)]
        pd = [E.ps(c, "f_pd0", [128, 512], F32),
              E.ps(c, "f_pd1", [128, 512], F32)]

        sw = E.sem("sw_l1f")
        swg = [E.sem("swg0"), E.sem("swg1")]
        sst = E.sem("sst")
        smm = E.sem("smm")
        sev = E.sem("sev")
        sac = E.sem("sac")
        sxp = [E.sem("sxp0"), E.sem("sxp1")]
        lna, lnd = E.sem("lna"), E.sem("lnd")

        for hc in range(HC3):
            E.dma("gpsimd", wd[:, hc * 1024:(hc + 1) * 1024],
                  W["l1_wdt"][hc * 128:(hc + 1) * 128, :], inc=sw)
        wd_ready = sw.val

        def load_wgu(g2, h, slot, wait):
            E.dma("gpsimd",
                  wg_b[slot][:].rearrange("p (c k) -> p c k", c=DC),
                  W["l1_wgt"][h].rearrange("(c p) k -> p c k", p=128),
                  wait=wait, inc=swg[slot])
            E.dma("gpsimd",
                  wu_b[slot][:].rearrange("p (c k) -> p c k", c=DC),
                  W["l1_wut"][h].rearrange("(c p) k -> p c k", p=128),
                  wait=wait, inc=swg[slot])
            return (swg[slot], swg[slot].val)

        tr_ev = [None, None]
        pgu_free = [(None, 0), (None, 0)]
        pd_free = [(None, 0), (None, 0)]
        wgu_used = [0, 0]
        silu_prev = 0
        ln_prev = [0, 0]
        ln_gate = 0
        xg_free = 0
        st_gate = 0

        for g2 in range(NG2):
            ln_apply = []
            for sub in range(2):
                for tt in range(4):
                    gt = 8 * g2 + 4 * sub + tt
                    wl = []
                    if gt >= 4:
                        wl.append((lnd, ln_gate))
                    if g2 > 0:
                        wl.append((sst, xg_free))
                    E.dma("sync", xg[:, tt * 1024:(tt + 1) * 1024],
                          tile(x_src, gt), wait=wl, inc=sxp[0])
                x_ready = sxp[0].val
                for tt in range(4):
                    xa = xg[:, tt * 1024:(tt + 1) * 1024]
                    na = xng[:, tt * 1024:(tt + 1) * 1024]
                    _, d2 = emit_ln_tile(E, C, st, tt, xa, xn_out=na,
                                         wait_x=((sxp[0], x_ready),))
                    ln_apply.append(d2)
                ln_gate = E.sem("lnd").val
                for ci in range(DC):
                    pt = ptr[ci % 2]
                    for tt in range(4):
                        src = xng[:, tt * 1024 + ci * 128:
                                  tt * 1024 + (ci + 1) * 128]
                        dst = pt[:, tt * 128:(tt + 1) * 128]
                        wl = [(lnd, ln_apply[sub * 4 + tt])]
                        if g2 == 0 and sub == 0 and ci < 2 and tt == 0:
                            wl.append(C["ready"])
                        if tr_ev[ci % 2]:
                            wl.append(tr_ev[ci % 2])
                        E.emit("tensor", lambda e, d=dst, s=src: e.transpose(
                            d, s, C["ident"][:]), wait=wl,
                            inc=(smm if tt == 3 else None))
                    mv = smm.val
                    dst = xnt[:, ci * 1024 + sub * 512: ci * 1024 + (sub + 1) * 512]
                    tr_ev[ci % 2] = (sac, E.emit(
                        "scalar", lambda e, d=dst, pt=pt: e.activation(
                            d, pt[:, 0:512], AF.Copy),
                        wait=((smm, mv),), inc=sac))
            xnt_ready = sac.val

            w_rdy = load_wgu(g2, 0, 0, ((smm, wgu_used[0]),))
            for h in range(HC3):
                hp = h % 2
                if h + 1 < HC3:
                    nx = load_wgu(g2, h + 1, (h + 1) % 2,
                                  ((smm, wgu_used[(h + 1) % 2]),))
                for sub in range(2):
                    for which, wb, pp, pfree in (
                            ("g", wg_b[hp], pg[sub], pgu_free[sub]),
                            ("u", wu_b[hp], pu[sub], pgu_free[sub])):
                        for ci in range(DC):
                            lhs = wb[:, ci * 128:(ci + 1) * 128]
                            rhs = xnt[:, ci * 1024 + sub * 512:
                                      ci * 1024 + (sub + 1) * 512]
                            wl = []
                            if ci == 0:
                                wl = [(sac, xnt_ready), w_rdy]
                                if which == "g" and pfree[0] is not None:
                                    wl.append(pfree)
                            E.emit("tensor", lambda e, p=pp, l=lhs, r=rhs,
                                   ci=ci: e.matmul(p[:], l, r,
                                                   start=(ci == 0),
                                                   stop=(ci == DC - 1)),
                                   wait=wl,
                                   inc=(smm if ci == DC - 1 else None))
                    mv = smm.val
                    wgu_used[hp] = mv
                    a = E.emit("scalar", lambda e, p=pg[sub]: e.activation(
                        tmps[:], p[:], AF.Silu, scale=C["dq"]),
                        wait=((smm, mv), (sev, silu_prev)), inc=sac)
                    dv = E.emit("vector", lambda e, p=pu[sub], h=h, sub=sub:
                                e.scalar_tensor_tensor(
                                    hfm[:, h * 1024 + sub * 512:
                                        h * 1024 + (sub + 1) * 512],
                                    p[:], 1.0 / 127.0, tmps[:],
                                    ALU.mult, ALU.mult),
                                wait=((sac, a),), inc=sev)
                    pgu_free[sub] = (sev, dv)
                    silu_prev = dv
                w_rdy = nx if h + 1 < HC3 else w_rdy
            hfm_ready = sev.val

            for sub in range(2):
                if sub == 0:
                    wl0 = [(lnd, ln_gate), (lna, E.sem("lna").val)]
                    if g2 > 0 or True:
                        wl0.append((sst, xg_free))
                else:
                    wl0 = [(sst, xg_free)]
                for tt in range(4):
                    gt = 8 * g2 + 4 * sub + tt
                    E.dma("sync", xg[:, tt * 1024:(tt + 1) * 1024],
                          tile(x_src, gt),
                          wait=(wl0 if tt == 0 else ()), inc=sxp[1])
                for tt in range(4):
                    gt = 8 * g2 + 4 * sub + tt
                    h2 = tt % 2
                    of = outf[:, h2 * 1024:(h2 + 1) * 1024]
                    ofr = tmpr[:, h2 * 1024:(h2 + 1) * 1024]
                    ev_out = []
                    for oc in range(2):
                        p = pd[oc % 2]
                        for h in range(HC3):
                            lhs = hfm[:, h * 1024 + sub * 512 + tt * 128:
                                      h * 1024 + sub * 512 + (tt + 1) * 128]
                            rhs = wd[:, h * 1024 + oc * 512:
                                     h * 1024 + (oc + 1) * 512]
                            wl = []
                            if h == 0:
                                wl = [(sev, hfm_ready), (sw, wd_ready)]
                                if pd_free[oc % 2][0] is not None:
                                    wl.append(pd_free[oc % 2])
                            E.emit("tensor", lambda e, p=p, l=lhs, r=rhs,
                                   h=h: e.matmul(p[:], l, r,
                                                 start=(h == 0),
                                                 stop=(h == HC3 - 1)),
                                   wait=wl,
                                   inc=(smm if h == HC3 - 1 else None))
                        mv = smm.val
                        wlv = [(smm, mv)]
                        if ln_prev[h2]:
                            wlv.append((lna, ln_prev[h2]))
                        ev = E.emit("scalar", lambda e, d=of, oc=oc, p=p:
                                    e.activation(
                                        d[:, oc * 512:(oc + 1) * 512], p[:],
                                        AF.Copy, scale=C["dq"]),
                                    wait=wlv, inc=sac)
                        ev_out.append(ev)
                        pd_free[oc % 2] = (sac, ev)
                    a2, _ = emit_ln_tile(E, C, st, 4 + h2, of,
                                         wait_x=((sac, ev_out[-1]),))
                    ln_prev[h2] = a2
                    cs = slice(4 + h2, 5 + h2)
                    xa = xg[:, tt * 1024:(tt + 1) * 1024]
                    v = E.emit("vector", lambda e, d=ofr, s=of, cs=cs:
                               e.tensor_scalar(d, s, st["mu"][:, cs],
                                               st["rstd"][:, cs],
                                               ALU.subtract, ALU.mult),
                               wait=((lna, a2), (lnd, E.sem("lnd").val)),
                               inc=sev)
                    rv = E.emit("vector", lambda e, xa=xa, d=ofr:
                                e.tensor_add(xa, xa, d),
                                wait=((sev, v), (sxp[1], sxp[1].val)),
                                inc=sev)
                    E.dma("sync", tile(x_dst, gt), xa,
                          wait=((sev, rv),), inc=sst)
                xg_free = sst.val



BLOCKS_ALL = ("l0_attn", "l0_moe", "l1_attn", "l1_ffn")


def build_program(blocks=BLOCKS_ALL):
    nc = bass.Bass("TRN2", target_bir_lowering=False)
    dram = {}
    dram["x_in"] = nc.dram_tensor("x_in", [T, D], F32, kind="ExternalInput").ap()
    dram["l0_wq"] = nc.dram_tensor("l0_wq", [D, D], F32, kind="ExternalInput").ap()
    dram["l0_wk"] = nc.dram_tensor("l0_wk", [D, D], F32, kind="ExternalInput").ap()
    dram["l0_wv"] = nc.dram_tensor("l0_wv", [D, D], F32, kind="ExternalInput").ap()
    dram["l0_wo"] = nc.dram_tensor("l0_wo", [D, D], F32, kind="ExternalInput").ap()
    dram["moe_w1t"] = nc.dram_tensor("moe_w1t", [10, D, 2048], F32, kind="ExternalInput").ap()
    dram["moe_w3t"] = nc.dram_tensor("moe_w3t", [10, D, 2048], F32, kind="ExternalInput").ap()
    dram["moe_w2t"] = nc.dram_tensor("moe_w2t", [10, 2048, D], F32, kind="ExternalInput").ap()
    dram["router_t"] = nc.dram_tensor("router_t", [D, 8], F32, kind="ExternalInput").ap()
    dram["router_bias"] = nc.dram_tensor("router_bias", [1, 8], F32, kind="ExternalInput").ap()
    dram["l1_wqkv"] = nc.dram_tensor("l1_wqkv", [D, 1536], F32, kind="ExternalInput").ap()
    dram["l1_wout"] = nc.dram_tensor("l1_wout", [512, D], F32, kind="ExternalInput").ap()
    dram["l1_lnw"] = nc.dram_tensor("l1_lnw", [128, 8], F32, kind="ExternalInput").ap()
    dram["l1_lnb"] = nc.dram_tensor("l1_lnb", [128, 8], F32, kind="ExternalInput").ap()
    dram["l1_wgt"] = nc.dram_tensor("l1_wgt", [24, D, 128], I32, kind="ExternalInput").ap()
    dram["l1_wut"] = nc.dram_tensor("l1_wut", [24, D, 128], I32, kind="ExternalInput").ap()
    dram["l1_wdt"] = nc.dram_tensor("l1_wdt", [3072, D], I32, kind="ExternalInput").ap()
    dram["x_out"] = nc.dram_tensor("x_out", [T, D], F32, kind="ExternalOutput").ap()
    dram["xa"] = nc.dram_tensor("xa", [T, D], F32, kind="Internal").ap()
    dram["xb"] = nc.dram_tensor("xb", [T, D], F32, kind="Internal").ap()
    dram["moe_acc"] = nc.dram_tensor("moe_acc", [T, D], F32, kind="Internal").ap()
    dram["xnt_spill"] = nc.dram_tensor("xnt_spill", [D, T], BF16, kind="Internal").ap()

    with ExitStack() as ctx:
        E = Env(nc, ctx)
        C = emit_consts(E, ctx)

        chain = [dram["x_in"]]
        for i in range(len(blocks) - 1):
            chain.append(dram["xa"] if i % 2 == 0 else dram["xb"])
        chain.append(dram["x_out"])

        for i, b in enumerate(blocks):
            src, dst = chain[i], chain[i + 1]
            if b == "l0_attn":
                emit_l0_attn(E, C, src, dst, dram)
            elif b == "l0_moe":
                emit_l0_moe(E, C, src, dst, dram)
            elif b == "l1_attn":
                emit_l1_attn(E, C, src, dst, dram)
            elif b == "l1_ffn":
                emit_l1_ffn(E, C, src, dst, dram)
            else:
                raise ValueError(b)
            E.barrier()

        sst = E.sem("sst")
        E.emit("sync", lambda e: e.nop(), wait=((sst, sst.val),))
        assert E.check_deadlock(), "semaphore deadlock in emitted program"
        E.replay()
    return nc



def prep_feeds(inp):
    cnt = np.ascontiguousarray
    f = {}
    f["l0_wq"] = cnt(np.asarray(inp["L0_attn_Wq"]).T)
    f["l0_wk"] = cnt(np.asarray(inp["L0_attn_Wk"]).T)
    f["l0_wv"] = cnt(np.asarray(inp["L0_attn_Wv"]).T)
    f["l0_wo"] = cnt(np.asarray(inp["L0_attn_Wo"]).T)
    w1 = np.concatenate([np.asarray(inp["L0_ffn_w1"]),
                         np.asarray(inp["L0_ffn_sw1"])], axis=0)
    w3 = np.concatenate([np.asarray(inp["L0_ffn_w3"]),
                         np.asarray(inp["L0_ffn_sw3"])], axis=0)
    w2 = np.concatenate([np.asarray(inp["L0_ffn_w2"]),
                         np.asarray(inp["L0_ffn_sw2"])], axis=0)
    f["moe_w1t"] = cnt(w1.transpose(0, 2, 1))
    f["moe_w3t"] = cnt(w3.transpose(0, 2, 1))
    f["moe_w2t"] = cnt(w2.transpose(0, 2, 1))
    f["router_t"] = cnt(np.asarray(inp["L0_ffn_router_w"]).T)
    f["router_bias"] = cnt(np.asarray(inp["L0_ffn_expert_bias"]).reshape(1, 8))
    f["l1_wqkv"] = cnt(np.asarray(inp["L1_attn_w_qkv"]))
    f["l1_wout"] = cnt(np.asarray(inp["L1_attn_w_out"]))
    f["l1_lnw"] = cnt(np.asarray(inp["L1_attn_ln_w"]).reshape(8, 128).T)
    f["l1_lnb"] = cnt(np.asarray(inp["L1_attn_ln_b"]).reshape(8, 128).T)
    wgt = np.asarray(inp["L1_ffn_w_gate"]).T
    wut = np.asarray(inp["L1_ffn_w_up"]).T
    f["l1_wgt"] = cnt(wgt.reshape(1024, 24, 128).transpose(1, 0, 2))
    f["l1_wut"] = cnt(wut.reshape(1024, 24, 128).transpose(1, 0, 2))
    f["l1_wdt"] = cnt(np.asarray(inp["L1_ffn_w_down"]).T)
    return f


def kernel(**inp):
    x = np.asarray(inp["L0_attn_x"])
    feeds = prep_feeds(inp)
    nc = build_program()
    in_maps = []
    for i in range(8):
        m = dict(feeds)
        m["x_in"] = np.ascontiguousarray(x[i])
        in_maps.append(m)
    res = run_bass_kernel_spmd(nc, in_maps, core_ids=list(range(8)))
    return np.stack([r["x_out"] for r in res.results], axis=0)
